# version 01 (final)
# baseline (speedup 1.0000x reference)
import numpy as np
import ml_dtypes
from contextlib import ExitStack

import concourse.bass as bass
import concourse.mybir as mybir
from concourse.bass_utils import run_bass_kernel_spmd

F32 = mybir.dt.float32
BF16 = mybir.dt.bfloat16
AF = mybir.ActivationFunctionType
ALU = mybir.AluOpType
AX = mybir.AxisListType

B, T, D = 4, 2048, 2048
NH, NKV, HD = 32, 8, 64
E, M = 8, 4096
TQ = 1024
NQT = TQ // 128
NCT = T // 128
KC = D // 128
MC = M // 128
ROPE_BASE = 10000.0
EPS = 1e-5

bO_I128 = 0
bO_PERM = 128
bO_M = 256
bO_M_CC = bO_M
bO_M_SS = bO_M + 2048
bO_M_WV = bO_M + 4096
bO_M_MASK = bO_M
bO_M_SA = bO_M + 8192
bO_M_SB = bO_M + 10240
bO_M_W0 = [bO_M, bO_M + 2048]
bO_M_W1 = [bO_M + 4096, bO_M + 6144]
bO_M_WO = [bO_M + 8192, bO_M + 10240]
bO_X = 12544
bO_X_GRP = [bO_X, bO_X + 8192]
bO_X_WK = bO_X + 16384
bO_X_Q = bO_X
bO_X_CTX = bO_X + 16384
bO_X_H = bO_X
bO_K = 45312
bO_V = 53504
bO_XNQ = 61824
bO_T = 78208
bO_T_QP = bO_T
bO_T_PP = bO_T + 512
bO_T_RA = bO_T + 1024
bO_T_PM1 = bO_T + 1536
bO_T_C0 = bO_T + 2048
bO_T_C1 = bO_T + 2560
bO_T_CQ = bO_T + 1536
bO_T_SQ = bO_T + 2560
bO_T_TB = [bO_T + 1024, bO_T + 2048]
bO_T_CWB = bO_T + 3072
bO_END = 82304

fO_I128 = 0
fO_GATE = 128
fO_STAG = [256, 2304]
fO_XNF = 4352
fO_EV = [6400, 7424]
fO_STT = 8448
fO_LGTM = 8512
fO_RTT = 8576
fO_CW = 8640
fO_LGF = 8704
fO_TS = [9728, 9856]
fO_END = 9984

SEQ_NAMES = ["sync", "gpsimd", "tensor", "vector", "scalar"]

_HEADS_R0 = [h for h in range(NH) if (h // 4) % 2 == 0]
_HEADS_R64 = [h for h in range(NH) if (h // 4) % 2 == 1]
QSTRIP = {}
for _j, _h in enumerate(_HEADS_R0):
    QSTRIP[_h] = (_j, 0)
for _j, _h in enumerate(_HEADS_R64):
    QSTRIP[_h] = (_j, 64)


def _qpos(parity):
    return np.r_[0:512, 1536:2048] if parity == 0 else np.r_[512:1536]


def _host_consts():
    theta = 1.0 / ROPE_BASE ** (np.arange(0, HD, 2, dtype=np.float64) / HD)
    perm = np.zeros((128, 128), dtype=np.float32)
    for blk in range(2):
        for i in range(32):
            perm[blk * 64 + i, blk * 64 + 32 + i] = 1.0
            perm[blk * 64 + 32 + i, blk * 64 + i] = 1.0

    def tables(pos, scale):
        ang = np.outer(theta, pos.astype(np.float64))
        c = (np.cos(ang) * scale).astype(np.float32)
        s = (np.sin(ang) * scale).astype(np.float32)
        return (np.concatenate([c, c, c, c], 0),
                np.concatenate([-s, s, -s, s], 0))

    return perm, tables


def _masks(parity):
    qp = _qpos(parity)
    mk = np.zeros((16, 128, 512), dtype=np.float32)
    for mi in range(16):
        g = 0 if mi < 8 else 1
        kt = mi
        qcols = qp[g * 512:(g + 1) * 512]
        krows = kt * 128 + np.arange(128)
        mk[mi] = (krows[:, None] <= qcols[None, :]).astype(np.float32)
    return mk


class _StopEmit(Exception):
    pass


def build_nc(limit=10**9, detect_races=True, dbg=False):
    nc = bass.Bass("TRN2", target_bir_lowering=False, debug=False,
                   dynamic_dma_scratch_size=16384,
                   detect_race_conditions=detect_races)

    def din(name, shape, dt=F32):
        return nc.dram_tensor(name, shape, dt, kind="ExternalInput").ap()

    xq_d = din("xq", [TQ, D])
    xc_d = din("xc", [T, D])
    wq_d = din("wq_t", [16, 128, 16, 128])
    wk_d = din("wk_t", [4, 128, 16, 128])
    wv_d = din("wv", [16, 128, 512])
    wo_d = din("wo_t", [16, 128, 16, 128])
    gate_d = din("gate", [16, 128, 8])
    w0_d = din("w0_t", [E, 32, 128, 16, 128])
    w1_d = din("w1_t", [E, 32, 128, 16, 128])
    woe_d = din("woe_t", [E, 16, 2, 128, 16, 128])
    cq_d = din("cq", [128, TQ])
    sq_d = din("sq", [128, TQ])
    cc_d = din("cc", [128, T])
    ss_d = din("ss", [128, T])
    perm_d = din("perm", [128, 128])
    i128_d = din("i128", [128, 128])
    i8_d = din("i8", [8, 8])
    ones_d = din("ones128", [1, 128])
    eps_d = din("eps", [128, 1])
    mask_d = din("masks", [16, 128, 512])

    y_d = nc.dram_tensor("y", [TQ, D], F32, kind="ExternalOutput").ap()
    acc_d = nc.dram_tensor("accd", [D, TQ], F32,
                           kind="ExternalOutput" if dbg else "Internal").ap()
    if dbg:
        dbg_d = nc.dram_tensor("dbg", [128, 49152], BF16, kind="ExternalOutput").ap()
        dbg2_d = nc.dram_tensor("dbg2", [128, 512], F32, kind="ExternalOutput").ap()

    es = ExitStack()
    ABF = es.enter_context(nc.sbuf_tensor("ABF", [128, bO_END], BF16))
    AFX = es.enter_context(nc.sbuf_tensor("AFX", [128, fO_END], F32))
    i8_sb = es.enter_context(nc.sbuf_tensor("i8_sb", [8, 8], F32))
    eps_sb = es.enter_context(nc.sbuf_tensor("eps_sb", [128, 1], F32))
    ones_sb = es.enter_context(nc.sbuf_tensor("ones_sb", [1, 128], BF16))
    cwrow_sb = es.enter_context(nc.sbuf_tensor("cwrow_sb", [1, TQ], BF16))
    drf_sb = es.enter_context(nc.sbuf_tensor("drf_sb", [1, 512], F32))
    drb_sb = es.enter_context(nc.sbuf_tensor("drb_sb", [1, 512], BF16))
    pp = [es.enter_context(nc.psum_tensor(f"pp{i}", [128, 1024], F32)) for i in range(4)]

    SEM = {}
    for s in ["lds", "ldg", "ga", "sto", "mt", "sv", "se", "ch"]:
        SEM[s] = es.enter_context(nc.semaphore(s))
    C = {k: 0 for k in SEM}
    q = {k: [] for k in SEQ_NAMES}

    DMA_SEMS = {"lds": "sync", "sto": "sync", "ldg": "gpsimd", "ga": "gpsimd"}
    QUEUE_SEMS = {"sync": ["lds", "sto"], "gpsimd": ["ldg", "ga"]}
    milestones = {s: [0] for s in DMA_SEMS}

    def _mark(eng):
        for s in QUEUE_SEMS.get(eng, ()):
            if milestones[s][-1] != C[s]:
                milestones[s].append(C[s])

    def _resolve(sem, ndma):
        for m in milestones[sem]:
            if m >= ndma:
                return m
        return milestones[sem][-1]

    def bfv(off, ln, p0=0, p1=128):
        return ABF[p0:p1, off:off + ln]

    def ffv(off, ln, p0=0, p1=128):
        return AFX[p0:p1, off:off + ln]

    def wt(eng, sem, val):
        _mark(eng)
        if val and val > 0:
            q[eng].append(lambda Eg, s=SEM[sem], v=val: Eg.wait_ge(s, v))

    def wtd(eng, sem, ndma):
        _mark(eng)
        if ndma and ndma > 0:
            q[eng].append(lambda Eg, s=SEM[sem], sn=sem, n=ndma:
                          Eg.wait_ge(s, 16 * _resolve(sn, n)))

    def op(eng, fn, inc=None):
        if inc is None:
            q[eng].append(fn)
            return None
        C[inc] += 1
        tok = C[inc]
        q[eng].append(lambda Eg, fn=fn, s=SEM[inc]: fn(Eg).then_inc(s, 1))
        return tok

    def dma(eng, out, in_, inc, **kw):
        if eng == "gpsimd":
            wtd(eng, "ldg", C["ldg"])
            wtd(eng, "ga", C["ga"])
        C[inc] += 1
        tok = C[inc]
        q[eng].append(lambda Eg, o=out, i=in_, s=SEM[inc], kw=kw:
                      Eg.dma_start(out=o, in_=i, **kw).then_inc(s, 16))
        return tok

    def CH(eng, fn):
        wt(eng, "ch", C["ch"])
        return op(eng, fn, "ch")

    def barrier():
        snap = dict(C)
        for e in SEQ_NAMES:
            for s in ["mt", "sv", "se", "ch"]:
                wt(e, s, snap[s])
            for s in ["lds", "ldg", "ga", "sto"]:
                wtd(e, s, snap[s])
        return snap

    MM = dict(skip_group_check=True)

    gate_no = [0]

    def gate(n=None, tag=""):
        gate_no[0] += 1
        if limit < gate_no[0]:
            import sys
            print(f"STOPPED at gate {gate_no[0]} {tag}", file=sys.stderr)
            raise _StopEmit()

    dma("sync", ffv(fO_I128, 128), i128_d[:], "lds")
    dma("sync", i8_sb[:], i8_d[:], "lds")
    dma("sync", eps_sb[:], eps_d[:], "lds")
    dma("sync", ffv(fO_GATE, 128).rearrange("p (c e) -> p c e", e=8),
        gate_d.rearrange("c p e -> p c e"), "lds")
    dma("gpsimd", bfv(bO_I128, 128), i128_d[:], "ldg")
    dma("gpsimd", bfv(bO_PERM, 128), perm_d[:], "ldg")
    dma("gpsimd", ones_sb[:], ones_d[:], "ldg")
    dma("gpsimd", bfv(bO_M_CC, 2048), cc_d[:], "ldg")
    dma("gpsimd", bfv(bO_M_SS, 2048), ss_d[:], "ldg")
    dma("gpsimd", bfv(bO_M_WV, 8192).rearrange("p (c n) -> p c n", n=512),
        wv_d.rearrange("c p n -> p c n"), "ldg")
    for s in range(4):
        dma("gpsimd", bfv(bO_X_WK + s * 2048, 2048),
            wk_d[s].rearrange("p c n -> p (c n)"), "ldg")
    dma("gpsimd", bfv(bO_T_CQ, 1024), cq_d[:], "ldg")
    dma("gpsimd", bfv(bO_T_SQ, 1024), sq_d[:], "ldg")
    nconst_g = C["ldg"]

    op("vector", lambda Eg: Eg.memset(bfv(bO_V, 16 * 520), 1.0), "sv")

    tts_by_tile = {}

    def ln_tile(tile_key, stag_off, xnbf_out, xnf_out, stt_col, xtok):
        wtd("vector", "lds", xtok)
        for i in range(4):
            CH("vector", lambda Eg, i=i, so=stag_off, sc=stt_col:
               Eg.bn_stats(ffv(fO_STT + sc + 6 * i, 6), ffv(so + 512 * i, 512)))
        CH("vector", lambda Eg, sc=stt_col:
           Eg.bn_aggr(ffv(fO_STT + sc + 24, 2), ffv(fO_STT + sc, 24)))
        CH("scalar", lambda Eg, sc=stt_col:
           Eg.activation(ffv(fO_STT + sc + 26, 1), ffv(fO_STT + sc + 25, 1), AF.Sqrt,
                         bias=eps_sb[:, 0:1]))
        CH("vector", lambda Eg, sc=stt_col:
           Eg.reciprocal(ffv(fO_STT + sc + 27, 1), ffv(fO_STT + sc + 26, 1)))
        CH("vector", lambda Eg, sc=stt_col:
           Eg.scalar_tensor_tensor(ffv(fO_STT + sc + 28, 1), ffv(fO_STT + sc + 24, 1),
                                   -1.0, ffv(fO_STT + sc + 27, 1), ALU.mult, ALU.mult))
        wt("vector", "mt", C["mt"])
        if xnf_out is not None:
            CH("vector", lambda Eg, sc=stt_col, so=stag_off, xo=xnf_out:
               Eg.tensor_scalar(ffv(xo, 2048), ffv(so, 2048), ffv(fO_STT + sc + 27, 1),
                                ffv(fO_STT + sc + 28, 1), ALU.mult, ALU.add))
            tts = CH("scalar", lambda Eg, xo=xnf_out, bo=xnbf_out:
                     Eg.copy(bfv(bo, 2048), ffv(xo, 2048)))
        else:
            tts = CH("vector", lambda Eg, sc=stt_col, so=stag_off, bo=xnbf_out:
                     Eg.tensor_scalar(bfv(bo, 2048), ffv(so, 2048), ffv(fO_STT + sc + 27, 1),
                                      ffv(fO_STT + sc + 28, 1), ALU.mult, ALU.add))
        tts_by_tile[tile_key] = tts
        return tts

    def rope(psrc, tabc_off, tabs_off, dst_off, wait_mt_tok):
        wt("scalar", "mt", wait_mt_tok)
        t_pre = op("scalar", lambda Eg, p=psrc: Eg.copy(bfv(bO_T_QP, 512), p), "se")
        wt("tensor", "se", t_pre)
        t_pm = op("tensor", lambda Eg:
                  Eg.matmul(pp[3][:, 512:1024], bfv(bO_PERM, 128), bfv(bO_T_QP, 512),
                            start=True, stop=True, **MM), "mt")
        wt("vector", "se", t_pre)
        op("vector", lambda Eg, c0=tabc_off:
           Eg.tensor_mul(bfv(bO_T_RA, 512), bfv(bO_T_QP, 512), bfv(c0, 512)), None)
        wt("scalar", "mt", t_pm)
        t_pp = op("scalar", lambda Eg: Eg.copy(bfv(bO_T_PP, 512), pp[3][:, 512:1024]), "se")
        wt("vector", "se", t_pp)
        op("vector", lambda Eg, s0=tabs_off:
           Eg.tensor_mul(bfv(bO_T_QP, 512), bfv(bO_T_PP, 512), bfv(s0, 512)), None)
        tok = op("vector", lambda Eg, d=dst_off:
                 Eg.tensor_add(bfv(d, 512), bfv(bO_T_QP, 512), bfv(bO_T_RA, 512)), "sv")
        return tok

    try:
        gate(tag="gate(2)")
        last_grp_mt = [0, 0]
        for cg in range(4):
            g_off = bO_X_GRP[cg % 2]
            grp_guard = last_grp_mt[cg % 2]
            for tl in range(4):
                ti = cg * 4 + tl
                so = fO_STAG[ti % 2]
                if ti >= 2:
                    wt("sync", "ch", tts_by_tile[("c", ti - 2)])
                xtok = dma("sync", ffv(so, 2048), xc_d[ti * 128:(ti + 1) * 128, :], "lds")
                tts = ln_tile(("c", ti), so, bO_XNQ, None, 32 * (ti % 2), xtok)
                gate(tag="gate(1.1 + ti * 0.001)")
                for c in range(KC):
                    par = c % 2
                    pslot = pp[0][:, par * 512:par * 512 + 128]
                    wt("tensor", "ch", tts)
                    wt("tensor", "se", C["se"] - 1)
                    tmm = op("tensor", lambda Eg, c=c, ps=pslot:
                             Eg.matmul(ps, bfv(bO_XNQ + c * 128, 128), bfv(bO_I128, 128),
                                       start=True, stop=True, **MM), "mt")
                    wt("scalar", "mt", tmm)
                    wt("scalar", "mt", grp_guard)
                    op("scalar", lambda Eg, ps=pslot, c=c, go=g_off, tl=tl:
                       Eg.copy(bfv(go + c * 512 + tl * 128, 128), ps), "se")
            gate(tag="gate(1.2 + cg * 0.01)")
            grp_ready = C["se"]
            wtd("tensor", "ldg", nconst_g)
            wt("tensor", "se", grp_ready)
            for s in range(4):
                psK = pp[2][:, 0:512]
                tk = None
                for c in range(KC):
                    tk = op("tensor", lambda Eg, s=s, c=c, go=g_off, ps=psK:
                            Eg.matmul(ps, bfv(bO_X_WK + s * 2048 + c * 128, 128),
                                      bfv(go + c * 512, 512),
                                      start=(c == 0), stop=(c == KC - 1), **MM),
                            "mt" if c == KC - 1 else None)
                gate(tag="gate(1.3 + cg * 0.01 + s * 0.001)")
                rope(psK, bO_M_CC + cg * 512, bO_M_SS + cg * 512,
                     bO_K + s * 2048 + cg * 512, tk)
                gate(tag="gate(1.35 + cg * 0.01 + s * 0.001)")
                wt("tensor", "se", C["se"])
            for tl in range(4):
                ti = cg * 4 + tl
                psV = pp[3][:, 0:512]
                tv = None
                for c in range(KC):
                    tv = op("tensor", lambda Eg, c=c, go=g_off, tl=tl, ps=psV:
                            Eg.matmul(ps, bfv(go + c * 512 + tl * 128, 128),
                                      bfv(bO_M_WV + c * 512, 512),
                                      start=(c == 0), stop=(c == KC - 1), **MM),
                            "mt" if c == KC - 1 else None)
                wt("vector", "mt", tv)
                tve = op("vector", lambda Eg, ti=ti, ps=psV:
                         Eg.tensor_copy(
                             bfv(bO_V + ti * 520, 520).rearrange("p (v c) -> p v c", c=65)[:, :, 0:64],
                             ps.rearrange("p (v c) -> p v c", c=64)), "sv")
                wt("tensor", "sv", tve)
                gate(tag="gate(1.4 + cg * 0.01 + tl * 0.001)")
            last_grp_mt[cg % 2] = C["mt"]
        p2_end = barrier()

        gate(tag="gate(3)")
        mask_toks = []
        for i in range(2):
            mask_toks.append(dma("gpsimd",
                             bfv(bO_M_MASK + i * 4096, 4096).rearrange("p (m c) -> p m c", c=512),
                             mask_d[i * 8:(i + 1) * 8].rearrange("m p c -> p m c"), "ldg"))
            gate(tag=f"p3 mask dma {i}")
        wq_toks = [dma("gpsimd", bfv(bO_M_SA if ns % 2 == 0 else bO_M_SB, 2048),
                       wq_d[ns].rearrange("p c n -> p (c n)"), "ldg") for ns in range(2)]

        gate(tag="p3 after mask/wq dmas")
        XN3 = bO_X_CTX
        for qt in range(NQT):
            so = fO_STAG[qt % 2]
            if qt >= 2:
                wt("sync", "ch", tts_by_tile[("q", qt - 2)])
            xtok = dma("sync", ffv(so, 2048), xq_d[qt * 128:(qt + 1) * 128, :], "lds")
            tts = ln_tile(("q", qt), so, XN3, fO_XNF, 32 * (qt % 2), xtok)
            gate(tag=f"p3 ln qt")
            for c in range(KC):
                par = c % 2
                psTb = pp[0][:, par * 512:par * 512 + 128]
                psTf = pp[1][:, par * 512:par * 512 + 128]
                wt("tensor", "ch", tts)
                wt("tensor", "se", C["se"] - 1)
                tmm = op("tensor", lambda Eg, c=c, ps=psTb:
                         Eg.matmul(ps, bfv(XN3 + c * 128, 128), bfv(bO_I128, 128),
                                   start=True, stop=True, **MM), "mt")
                wt("scalar", "mt", tmm)
                op("scalar", lambda Eg, ps=psTb, c=c, qt=qt:
                   Eg.copy(bfv(bO_XNQ + c * 1024 + qt * 128, 128), ps), "se")
                gate(tag=f"p3 bf16T qt{qt} c{c}")
                wt("tensor", "sv", C["sv"] - 1)
                tmf = op("tensor", lambda Eg, c=c, ps=psTf:
                         Eg.matmul(ps, ffv(fO_XNF + c * 128, 128), ffv(fO_I128, 128),
                                   start=True, stop=True, **MM), "mt")
                wt("vector", "mt", tmf)
                tse = op("vector", lambda Eg, ps=psTf, c=c:
                         Eg.tensor_copy(ffv(fO_TS[c % 2], 128), ps), "sv")
                gate(tag=f"p3 f32T qt{qt} c{c}")
                wt("tensor", "sv", tse)
                tLmm = op("tensor", lambda Eg, c=c:
                          Eg.matmul(pp[3][0:8, 512:640],
                                    ffv(fO_GATE + c * 8, 8), ffv(fO_TS[c % 2], 128),
                                    start=(c == 0), stop=(c == KC - 1), **MM),
                          "mt" if c == KC - 1 else None)
            wt("vector", "mt", tLmm)
            t_lgf = op("vector", lambda Eg, qt=qt:
                       Eg.tensor_copy(ffv(fO_LGF + qt * 128, 128, 0, 8),
                                      pp[3][0:8, 512:640]), "sv")
            wt("tensor", "sv", t_lgf)
        t_router_mm = C["mt"]
        for j in range(8):
            par = j % 2
            psLT = pp[0][:, par * 512:par * 512 + 8]
            wt("tensor", "sv", t_lgf)
            wt("tensor", "sv", C["sv"] - 1)
            tj = op("tensor", lambda Eg, j=j, ps=psLT:
                    Eg.matmul(ps, ffv(fO_LGF + j * 128, 128, 0, 8), i8_sb[:],
                              start=True, stop=True, **MM), "mt")
            wt("vector", "mt", tj)
            op("vector", lambda Eg, ps=psLT, j=j:
               Eg.tensor_copy(ffv(fO_LGTM + j * 8, 8), ps), "sv")
        t_lgtm = C["sv"]

        for j in range(8):
            wt("vector", "sv", t_lgtm)
            CH("vector", lambda Eg, j=j:
               Eg.reduce_max(ffv(fO_STT + 32, 1), ffv(fO_LGTM + j * 8, 8), axis=AX.X, negate=True))
            CH("vector", lambda Eg:
               Eg.tensor_scalar(ffv(fO_STT + 33, 1), ffv(fO_STT + 32, 1), -1.0, None, ALU.mult))
            CH("vector", lambda Eg, j=j:
               Eg.tensor_scalar(ffv(fO_RTT, 8), ffv(fO_LGTM + j * 8, 8), ffv(fO_STT + 33, 1),
                                None, ALU.is_equal))
            CH("vector", lambda Eg, j=j:
               Eg.scalar_tensor_tensor(ffv(fO_RTT + 8, 8), ffv(fO_RTT, 8), -1e30,
                                       ffv(fO_LGTM + j * 8, 8), ALU.mult, ALU.add))
            CH("vector", lambda Eg:
               Eg.reduce_max(ffv(fO_STT + 34, 1), ffv(fO_RTT + 8, 8), axis=AX.X))
            CH("vector", lambda Eg, j=j:
               Eg.tensor_scalar(ffv(fO_RTT + 16, 8), ffv(fO_LGTM + j * 8, 8), ffv(fO_STT + 34, 1),
                                None, ALU.is_ge))
            CH("scalar", lambda Eg, j=j:
               Eg.activation(ffv(fO_RTT + 24, 8), ffv(fO_LGTM + j * 8, 8), AF.Exp,
                             bias=ffv(fO_STT + 32, 1)))
            CH("scalar", lambda Eg:
               Eg.activation(ffv(fO_STT + 35, 1), ffv(fO_STT + 34, 1), AF.Exp,
                             bias=ffv(fO_STT + 32, 1)))
            CH("vector", lambda Eg:
               Eg.tensor_scalar(ffv(fO_STT + 36, 1), ffv(fO_STT + 35, 1), 1.0, None, ALU.add))
            CH("vector", lambda Eg:
               Eg.reciprocal(ffv(fO_STT + 37, 1), ffv(fO_STT + 36, 1)))
            CH("vector", lambda Eg:
               Eg.tensor_mul(ffv(fO_RTT + 32, 8), ffv(fO_RTT + 16, 8), ffv(fO_RTT + 24, 8)))
            CH("vector", lambda Eg, j=j:
               Eg.tensor_scalar(ffv(fO_CW + j * 8, 8), ffv(fO_RTT + 32, 8), ffv(fO_STT + 37, 1),
                                None, ALU.mult))
        t_cw = C["ch"]
        gate(tag="p3 after top2")

        cwt_sv = []
        for j in range(8):
            par = j % 2
            psCW = pp[1][0:8, par * 512:par * 512 + 128]
            wt("tensor", "ch", t_cw)
            if j >= 2:
                wt("tensor", "sv", cwt_sv[-2])
            tj = op("tensor", lambda Eg, j=j, ps=psCW:
                    Eg.matmul(ps, ffv(fO_CW + j * 8, 8), ffv(fO_I128, 128),
                              start=True, stop=True, **MM), "mt")
            wt("vector", "mt", tj)
            cwt_sv.append(op("vector", lambda Eg, ps=psCW, j=j:
                             Eg.tensor_copy(ffv(fO_LGF + j * 128, 128, 0, 8), ps), "sv"))
        t_cwt_done = cwt_sv[-1]
        gate(tag="p3 after cwrow")

        wt("tensor", "sv", t_lgf)
        for ns in range(16):
            slot = bO_M_SA if ns % 2 == 0 else bO_M_SB
            wtd("tensor", "ldg", wq_toks[ns])
            for half in range(2):
                psQ = pp[2][:, half * 512:(half + 1) * 512]
                tq_ = None
                for c in range(KC):
                    tq_ = op("tensor", lambda Eg, c=c, slot=slot, half=half, ps=psQ:
                             Eg.matmul(ps, bfv(slot + c * 128, 128),
                                       bfv(bO_XNQ + c * 1024 + half * 512, 512),
                                       start=(c == 0), stop=(c == KC - 1), **MM),
                             "mt" if c == KC - 1 else None)
                rope(psQ, bO_T_CQ + half * 512, bO_T_SQ + half * 512,
                     bO_X_Q + ns * 1024 + half * 512, tq_)
                wt("tensor", "se", C["se"])
            if ns + 2 < 16:
                wt("gpsimd", "mt", C["mt"])
                wq_toks.append(dma("gpsimd", bfv(slot, 2048),
                                   wq_d[ns + 2].rearrange("p c n -> p (c n)"), "ldg"))
            gate(tag=f"p3 qproj strip")
        p3_end = barrier()
        if dbg:
            dma("sync", dbg_d[:, 0:8448], bfv(0, 8448), "sto")
            dma("sync", dbg_d[:, 8448:24832], bfv(bO_XNQ, 16384), "sto")
            dma("sync", dbg_d[:, 24832:33024], bfv(bO_K, 8192), "sto")
            dma("sync", dbg_d[0:1, 33024:34048], cwrow_sb[0:1, :], "sto")
            dma("sync", dbg2_d[:, 0:128], ffv(fO_GATE, 128), "sto")
            dma("sync", dbg2_d[:, 128:192], ffv(fO_STT, 64), "sto")
            dma("sync", dbg2_d[:, 192:256], ffv(fO_LGTM, 64), "sto")
            dma("sync", dbg2_d[:, 256:320], ffv(fO_CW, 64), "sto")

        gate(tag="gate(4)")
        wo_toks = [dma("gpsimd", bfv(bO_M_SA if i % 2 == 0 else bO_M_SB, 2048),
                       wo_d[i].rearrange("p c n -> p (c n)"), "ldg") for i in range(2)]
        pP = [bO_T_QP, bO_T_PP]
        pPM = [bO_T_RA, bO_T_PM1]
        pCx = [bO_T_C0, bO_T_C1]
        for h in range(NH):
            v = h // 4
            ks, krow = v // 2, (v % 2) * 64
            qs, qrow = QSTRIP[h]
            cs, crow = h // 2, (h % 2) * 64
            for g in range(2):
                kts = 8 if g == 0 else 16
                psAV = pp[1][0:65, 0:512]
                for kt in range(kts):
                    par = kt % 2
                    psS = pp[0][:, par * 512:(par + 1) * 512]
                    wt("tensor", "se", C["se"] - 1)
                    tS = op("tensor", lambda Eg, ks=ks, krow=krow, kt=kt, qs=qs, qrow=qrow, g=g, ps=psS:
                            Eg.matmul(ps,
                                      bfv(bO_K + ks * 2048 + kt * 128, 128, krow, krow + 64),
                                      bfv(bO_X_Q + qs * 1024 + g * 512, 512, qrow, qrow + 64),
                                      start=True, stop=True, **MM), "mt")
                    wt("scalar", "mt", tS)
                    tE = op("scalar", lambda Eg, ps=psS, par=par:
                            Eg.activation(bfv(pP[par], 512), ps, AF.Exp), "se")
                    masked = (g == 0) or (kt >= 8)
                    if masked:
                        mi = kt
                        wt("vector", "se", tE)
                        tM = op("vector", lambda Eg, par=par, mi=mi:
                                Eg.tensor_mul(bfv(pPM[par], 512), bfv(pP[par], 512),
                                              bfv(bO_M_MASK + mi * 512, 512)), "sv")
                        wt("tensor", "sv", tM)
                        src = pPM[par]
                    else:
                        wt("tensor", "se", tE)
                        src = pP[par]
                    tAV_c = op("tensor", lambda Eg, kt=kt, v=v, src=src, kts=kts:
                               Eg.matmul(psAV, bfv(bO_V + kt * 520 + v * 65, 65), bfv(src, 512),
                                         start=(kt == 0), stop=(kt == kts - 1), **MM),
                               "mt" if kt == kts - 1 else None)
                tAV = tAV_c
                parC = g
                wt("scalar", "mt", tAV)
                tC = op("scalar", lambda Eg, parC=parC:
                        Eg.copy(bfv(pCx[parC], 512, 0, 64), pp[1][0:64, 0:512]), "se")
                wt("vector", "mt", tAV)
                tR = op("vector", lambda Eg:
                        Eg.reciprocal(drf_sb[0:1, :], pp[1][64:65, 0:512]), "sv")
                wt("scalar", "sv", tR)
                tD = op("scalar", lambda Eg: Eg.copy(drb_sb[0:1, :], drf_sb[0:1, :]), "se")
                wt("tensor", "se", tD)
                wt("tensor", "se", tC)
                tB = op("tensor", lambda Eg:
                        Eg.matmul(pp[2][0:64, 0:512], ones_sb[0:1, 0:64], drb_sb[0:1, :],
                                  start=True, stop=True, **MM), "mt")
                wt("vector", "mt", tB)
                wt("vector", "se", tC)
                tN = op("vector", lambda Eg, parC=parC, cs=cs, crow=crow, g=g:
                        Eg.tensor_mul(bfv(bO_X_CTX + cs * 1024 + g * 512, 512, crow, crow + 64),
                                      bfv(pCx[parC], 512, 0, 64), pp[2][0:64, 0:512]), "sv")
                wt("tensor", "sv", tN)
        p4_end = barrier()

        gate(tag="gate(5)")
        for nc_ in range(16):
            slot = bO_M_SA if nc_ % 2 == 0 else bO_M_SB
            wtd("tensor", "ldg", wo_toks[nc_])
            par = nc_ % 2
            psW = (pp[0] if par == 0 else pp[1])[:, 0:1024]
            wt("tensor", "sv", C["sv"] - 1)
            tW = None
            for c in range(KC):
                for half in range(2):
                    tW = op("tensor", lambda Eg, c=c, half=half, slot=slot, ps=psW:
                            Eg.matmul(ps[:, half * 512:(half + 1) * 512],
                                      bfv(slot + c * 128, 128),
                                      bfv(bO_X_CTX + c * 1024 + half * 512, 512),
                                      start=(c == 0), stop=(c == KC - 1), **MM),
                            "mt" if (c == KC - 1 and half == 1) else None)
            wt("vector", "mt", tW)
            if nc_ >= 2:
                wtd("vector", "ga", C["ga"] - 1)
            tEv = op("vector", lambda Eg, ps=psW, eslot=fO_EV[par]:
                     Eg.tensor_copy(ffv(eslot, 1024), ps), "sv")
            wt("gpsimd", "sv", tEv)
            dma("gpsimd", acc_d[nc_ * 128:(nc_ + 1) * 128, :], ffv(fO_EV[par], 1024), "ga")
            if nc_ + 2 < 16:
                wt("gpsimd", "mt", tW)
                wo_toks.append(dma("gpsimd", bfv(slot, 2048),
                                   wo_d[nc_ + 2].rearrange("p c n -> p (c n)"), "ldg"))
        p5_end = barrier()

        gate(tag="gate(6)")
        w0_toks, w1_toks, woe_toks = {}, {}, {}

        def pf_w01(e, m):
            w0_toks[(e, m)] = dma("gpsimd", bfv(bO_M_W0[m % 2], 2048),
                                  w0_d[e, m].rearrange("p c n -> p (c n)"), "ldg")
            w1_toks[(e, m)] = dma("gpsimd", bfv(bO_M_W1[m % 2], 2048),
                                  w1_d[e, m].rearrange("p c n -> p (c n)"), "ldg")

        def pf_woe(e, dc, mh):
            woe_toks[(e, dc, mh)] = dma("gpsimd", bfv(bO_M_WO[(dc * 2 + mh) % 2], 2048),
                                        woe_d[e, dc, mh].rearrange("p c n -> p (c n)"), "ldg")

        pf_w01(0, 0)
        pf_w01(0, 1)
        pf_woe(0, 0, 0)
        pf_woe(0, 0, 1)

        silu_se, hmul_sv = {}, {}
        tBC_prev = None
        for e in range(E):
            wt("gpsimd", "sv", t_cwt_done)
            if tBC_prev is not None:
                wt("gpsimd", "mt", tBC_prev)
            cw_tok = dma("gpsimd", cwrow_sb[0:1, :],
                         AFX[e:e + 1, fO_LGF:fO_LGF + 1024], "ldg")
            wtd("tensor", "ldg", cw_tok)
            tBC = None
            for half in range(2):
                tBC = op("tensor", lambda Eg, half=half:
                         Eg.matmul(pp[2][:, half * 512:(half + 1) * 512], ones_sb[0:1, :],
                                   cwrow_sb[0:1, half * 512:(half + 1) * 512],
                                   start=True, stop=True, **MM),
                         "mt" if half == 1 else None)
            tBC_prev = tBC
            wt("scalar", "mt", tBC)
            if e > 0:
                wt("scalar", "sv", hmul_sv[(e - 1, MC - 1)][1])
            tCWB = op("scalar", lambda Eg:
                      Eg.copy(bfv(bO_T_CWB, 1024), pp[2][:, 0:1024]), "se")
            for m in range(MC):
                par = m % 2
                psA = (pp[0] if par == 0 else pp[1])[:, 0:1024]
                psB = (pp[2] if par == 0 else pp[3])[:, 0:1024]
                wtd("tensor", "ldg", w1_toks[(e, m)])
                if m >= 2:
                    wt("tensor", "se", silu_se[(e, m - 2)])
                    wt("tensor", "sv", hmul_sv[(e, m - 2)][0])
                elif e > 0:
                    wt("tensor", "se", silu_se[(e - 1, MC - 2 + m)])
                    wt("tensor", "sv", hmul_sv[(e - 1, MC - 2 + m)][0])
                if m == 0:
                    wt("tensor", "se", tCWB)
                tA = None
                for c in range(KC):
                    for half in range(2):
                        tA = op("tensor", lambda Eg, c=c, half=half, m=m, ps=psA:
                                Eg.matmul(ps[:, half * 512:(half + 1) * 512],
                                          bfv(bO_M_W0[m % 2] + c * 128, 128),
                                          bfv(bO_XNQ + c * 1024 + half * 512, 512),
                                          start=(c == 0), stop=(c == KC - 1), **MM),
                                "mt" if (c == KC - 1 and half == 1) else None)
                tBt = None
                for c in range(KC):
                    for half in range(2):
                        tBt = op("tensor", lambda Eg, c=c, half=half, m=m, ps=psB:
                                 Eg.matmul(ps[:, half * 512:(half + 1) * 512],
                                           bfv(bO_M_W1[m % 2] + c * 128, 128),
                                           bfv(bO_XNQ + c * 1024 + half * 512, 512),
                                           start=(c == 0), stop=(c == KC - 1), **MM),
                                 "mt" if (c == KC - 1 and half == 1) else None)
                nm = m + 2
                ne, nm2 = (e, nm) if nm < MC else (e + 1, nm - MC)
                if ne < E:
                    wt("gpsimd", "mt", tBt)
                    pf_w01(ne, nm2)
                wt("scalar", "mt", tA)
                if m < 2:
                    wtd("scalar", "ga", C["ga"])
                silu_se[(e, m)] = op("scalar", lambda Eg, ps=psA, eslot=fO_EV[par]:
                                     Eg.activation(ffv(eslot, 1024), ps, AF.Silu), "se")
                wt("vector", "se", silu_se[(e, m)])
                wt("vector", "mt", tBt)
                t1 = op("vector", lambda Eg, ps=psB, eslot=fO_EV[par], par=par:
                        Eg.tensor_mul(bfv(bO_T_TB[par], 1024), ffv(eslot, 1024), ps), "sv")
                wt("vector", "se", tCWB)
                t2 = op("vector", lambda Eg, m=m, par=par:
                        Eg.tensor_mul(bfv(bO_X_H + m * 1024, 1024),
                                      bfv(bO_T_TB[par], 1024), bfv(bO_T_CWB, 1024)), "sv")
                hmul_sv[(e, m)] = (t1, t2)
            wt("tensor", "sv", hmul_sv[(e, MC - 1)][1])
            wt("tensor", "sv", hmul_sv[(e, MC - 2)][1])
            o_evac = {}
            for dc in range(16):
                par = dc % 2
                psC = (pp[0] if par == 0 else pp[1])[:, 0:1024]
                if dc >= 2:
                    wt("tensor", "sv", o_evac[dc - 2])
                tO = None
                for mh in range(2):
                    wtd("tensor", "ldg", woe_toks[(e, dc, mh)])
                    for ml in range(16):
                        mcg = mh * 16 + ml
                        for half in range(2):
                            tO = op("tensor", lambda Eg, mh=mh, ml=ml, mcg=mcg, half=half, dc=dc, ps=psC:
                                    Eg.matmul(ps[:, half * 512:(half + 1) * 512],
                                              bfv(bO_M_WO[(dc * 2 + mh) % 2] + ml * 128, 128),
                                              bfv(bO_X_H + mcg * 1024 + half * 512, 512),
                                              start=(mh == 0 and ml == 0),
                                              stop=(mh == 1 and ml == 15), **MM),
                                    "mt" if (ml == 15 and half == 1) else None)
                    nxt = dc * 2 + mh + 1
                    ne2, ndc, nmh = e, ((nxt + 1) // 2), ((nxt + 1) % 2)
                    if ndc >= 16:
                        ne2, ndc = e + 1, ndc - 16
                    if ne2 < E:
                        wt("gpsimd", "mt", tO)
                        pf_woe(ne2, ndc, nmh)
                wt("vector", "mt", tO)
                if dc >= 2 or e > 0:
                    wtd("vector", "ga", C["ga"] - 1 if dc >= 2 else C["ga"])
                tEv = op("vector", lambda Eg, ps=psC, eslot=fO_EV[par]:
                         Eg.tensor_copy(ffv(eslot, 1024), ps), "sv")
                o_evac[dc] = tEv
                wt("gpsimd", "sv", tEv)
                dma("gpsimd", acc_d[dc * 128:(dc + 1) * 128, :], ffv(fO_EV[par], 1024), "ga",
                    accum_op=ALU.add)
        p6_end = barrier()

        gate(tag="gate(7)")
        st_toks = []
        last_tt = {}
        for qt in range(NQT):
            sxq = fO_STAG[qt % 2]
            if qt >= 2:
                wt("sync", "sv", last_tt[qt - 2])
            xq_tok = dma("sync", ffv(sxq, 2048), xq_d[qt * 128:(qt + 1) * 128, :], "lds")
            oslot = fO_EV[0] if qt % 2 == 0 else fO_XNF
            tt_sv = {}
            acc_mm = {}
            for c in range(KC):
                par = c % 2
                stg = fO_TS[par]
                if (qt, c) != (0, 0) and (qt * KC + c) >= 2:
                    prev = (qt, c - 2) if c >= 2 else (qt - 1, KC - 2 + c)
                    wt("sync", "mt", acc_mm[prev] if prev in acc_mm else last_mm[prev])
                tok = dma("sync", ffv(stg, 128),
                          acc_d[c * 128:(c + 1) * 128, qt * 128:(qt + 1) * 128], "lds")
                psT = pp[2][:, par * 512:par * 512 + 128]
                wtd("tensor", "lds", tok)
                wt("tensor", "sv", C["sv"] - 1)
                tmm = op("tensor", lambda Eg, stg=stg, ps=psT:
                         Eg.matmul(ps, ffv(stg, 128), ffv(fO_I128, 128),
                                   start=True, stop=True, **MM), "mt")
                acc_mm[(qt, c)] = tmm
                wt("vector", "mt", tmm)
                wtd("vector", "lds", xq_tok)
                if qt >= 2 and c == 0:
                    wtd("vector", "sto", len(st_toks) - 1)
                tt_sv[c] = op("vector", lambda Eg, ps=psT, oslot=oslot, c=c, sxq=sxq:
                              Eg.tensor_add(ffv(oslot + c * 128, 128), ps,
                                            ffv(sxq + c * 128, 128)), "sv")
            last_mm = acc_mm
            last_tt[qt] = tt_sv[KC - 1]
            wt("sync", "sv", tt_sv[KC - 1])
            st_toks.append(dma("sync", y_d[qt * 128:(qt + 1) * 128, :], ffv(oslot, 2048), "sto"))
        wtd("sync", "sto", len(st_toks))
        wtd("gpsimd", "ga", C["ga"])


    except _StopEmit:
        pass
    for s in DMA_SEMS:
        if milestones[s][-1] != C[s]:
            milestones[s].append(C[s])

    block = es.enter_context(nc.Block())

    def mk_emitter(name):
        def f(Eg):
            for fn in q[name]:
                fn(Eg)
        return f

    block.sync(mk_emitter("sync"))
    block.gpsimd(mk_emitter("gpsimd"))
    block.tensor(mk_emitter("tensor"))
    block.vector(mk_emitter("vector"))
    block.scalar(mk_emitter("scalar"))
    es.close()
    return nc


def host_prepare(inputs):
    x = np.asarray(inputs["L0_attn_x"], dtype=np.float32)
    Wq = np.asarray(inputs["L0_attn_Wq"], dtype=np.float32)
    Wk = np.asarray(inputs["L0_attn_Wk"], dtype=np.float32)
    Wv = np.asarray(inputs["L0_attn_Wv"], dtype=np.float32)
    Wo = np.asarray(inputs["L0_attn_Wo"], dtype=np.float32)
    gate = np.asarray(inputs["L0_ffn_gate_w"], dtype=np.float32)
    w0 = np.asarray(inputs["L0_ffn_w0"], dtype=np.float32)
    w1 = np.asarray(inputs["L0_ffn_w1"], dtype=np.float32)
    woe = np.asarray(inputs["L0_ffn_wo"], dtype=np.float32)

    perm, tables = _host_consts()
    i128 = np.eye(128, dtype=np.float32)
    i8 = np.eye(8, dtype=np.float32)
    ones128 = np.ones((1, 128), dtype=np.float32)
    eps = np.full((128, 1), EPS, dtype=np.float32)
    horder = []
    for j in range(16):
        horder.append(_HEADS_R0[j])
        horder.append(_HEADS_R64[j])
    Wq_r = Wq.reshape(D, NH, HD)[:, horder, :].reshape(D, D)
    wq_t = np.ascontiguousarray(Wq_r.reshape(16, 128, 16, 128).transpose(2, 1, 0, 3))
    wk_t = np.ascontiguousarray(Wk.reshape(16, 128, 4, 128).transpose(2, 1, 0, 3))
    wv_r = np.ascontiguousarray(Wv.reshape(16, 128, 512))
    wo_t = np.ascontiguousarray(Wo.reshape(16, 128, 16, 128).transpose(2, 1, 0, 3))
    gate_r = np.ascontiguousarray(gate.reshape(16, 128, 8))
    w0_t = np.ascontiguousarray(w0.reshape(E, 16, 128, 32, 128).transpose(0, 3, 2, 1, 4))
    w1_t = np.ascontiguousarray(w1.reshape(E, 16, 128, 32, 128).transpose(0, 3, 2, 1, 4))
    woe_t = np.ascontiguousarray(
        woe.reshape(E, 2, 16, 128, 16, 128).transpose(0, 4, 1, 3, 2, 5))
    cc, ss = tables(np.arange(T), 1.0)

    in_maps = []
    for c in range(8):
        s, p = c // 2, c % 2
        qp = _qpos(p)
        cq, sq = tables(qp, 0.125)
        in_maps.append(dict(
            xq=np.ascontiguousarray(x[s][qp]),
            xc=np.ascontiguousarray(x[s]),
            wq_t=wq_t, wk_t=wk_t, wv=wv_r, wo_t=wo_t, gate=gate_r,
            w0_t=w0_t, w1_t=w1_t, woe_t=woe_t,
            cq=np.ascontiguousarray(cq), sq=np.ascontiguousarray(sq),
            cc=np.ascontiguousarray(cc), ss=np.ascontiguousarray(ss),
            perm=perm, i128=i128, i8=i8, ones128=ones128, eps=eps,
            masks=_masks(p),
        ))
    return in_maps


_nc_cache = None


def kernel(**inputs):
    global _nc_cache
    in_maps = host_prepare(inputs)
    if _nc_cache is None:
        _nc_cache = build_nc()
    res = run_bass_kernel_spmd(_nc_cache, in_maps, core_ids=list(range(8)))
    out = np.empty((B, T, D), dtype=np.float32)
    for c in range(8):
        s, p = c // 2, c % 2
        out[s][_qpos(p)] = res.results[c]["y"]
    return out
